# version 07
# baseline (speedup 1.0000x reference)
import math
from contextlib import ExitStack

import numpy as np
import concourse.bass as bass
import concourse.mybir as mybir
from concourse.bass_utils import run_bass_kernel_spmd

F32 = mybir.dt.float32
BF16 = mybir.dt.bfloat16
I32 = mybir.dt.int32
AF = mybir.ActivationFunctionType
ALU = mybir.AluOpType

ENGINES = ("T", "V", "S", "P", "Q")


class Prog:

    def __init__(self, nc):
        self.nc = nc
        self.es = ExitStack()
        self.eng = {
            "T": nc.tensor, "V": nc.vector, "S": nc.scalar,
            "P": nc.gpsimd, "Q": nc.sync,
        }
        self.sem = {}
        for e in ENGINES:
            self.sem[e] = self.es.enter_context(nc.semaphore(f"sem_{e}"))
        self.count = {k: 0 for k in self.sem}
        self.dma_chans = set()
        self.thunks = {e: [] for e in ENGINES}
        self.know = {e: {} for e in ENGINES}
        self.last_write = {}
        self.readers = {}
        self.guards = {}
        self.free_pool = []
        self.arena = [(self.ARENA_LO, self.ARENA_HI)]
        self.nops = 0

    ARENA_LO = 24576
    ARENA_HI = 229248

    def sbuf(self, name, shape, dtype=F32):
        esz = {F32: 4, BF16: 2, I32: 4}[dtype]
        nbytes = (int(np.prod(shape[1:])) * esz + 63) & ~63
        best = None
        for i, (lo, hi) in enumerate(self.arena):
            if hi - lo >= nbytes and (best is None or hi - lo < self.arena[best][1] - self.arena[best][0]):
                best = i
        if best is None:
            raise MemoryError(f"SBUF arena full allocating {name} ({nbytes}B): {self.arena}")
        lo, hi = self.arena[best]
        if hi - lo == nbytes:
            self.arena.pop(best)
        else:
            self.arena[best] = (lo + nbytes, hi)
        h = self.nc.alloc_sbuf_tensor_at(name, list(shape), dtype, offset=lo)
        self.guards[h.name] = (lo, nbytes)
        if self.free_pool:
            self.readers[h.name] = list(self.free_pool)
        return h

    def free(self, v):
        name = self.key_of(v)
        t = self.last_write.get(name)
        if t is not None:
            self.free_pool.append(t)
        self.free_pool.extend(self.readers.get(name, ()))
        best = {}
        for semk, val, ksnap in self.free_pool:
            if semk not in best or best[semk][1] < val:
                best[semk] = (semk, val, ksnap)
        self.free_pool = list(best.values())
        off, nbytes = self.guards.pop(name)
        self.arena.append((off, off + nbytes))
        self.arena.sort()
        merged = []
        for lo, hi in self.arena:
            if merged and merged[-1][1] == lo:
                merged[-1] = (merged[-1][0], hi)
            else:
                merged.append((lo, hi))
        self.arena = merged

    def psum(self, name, shape, dtype=F32):
        return self.es.enter_context(self.nc.psum_tensor(name, shape, dtype))

    @staticmethod
    def key_of(x):
        if isinstance(x, str):
            return x
        if isinstance(x, bass.AP):
            return x.tensor.name if hasattr(x, "tensor") else x.name()
        return x.name

    DUP_S = True

    def op(self, e, fn, reads=(), writes=(), dma_ch=None, inc=None, _dup=False):
        rkeys = [self.key_of(r) for r in reads]
        wkeys = [self.key_of(w) for w in writes]
        deps = []
        for k in rkeys:
            t = self.last_write.get(k)
            if t is not None:
                deps.append(t)
        for k in wkeys:
            t = self.last_write.get(k)
            if t is not None:
                deps.append(t)
            deps.extend(self.readers.get(k, ()))
        waits = []
        prev = self.count[e]
        if prev > self.know[e].get(e, 0):
            waits.append((e, prev))
            self.know[e][e] = prev
        for tok in deps:
            semk, val, ksnap = tok
            if semk in self.dma_chans:
                val = self.count[semk]
            cur = self.know[e].get(semk, 0)
            if cur < val:
                waits.append((semk, val))
                self.know[e][semk] = val
                for sk, sv in ksnap.items():
                    if self.know[e].get(sk, 0) < sv:
                        self.know[e][sk] = sv
        if dma_ch is not None:
            semk = "D:" + dma_ch
            if semk not in self.sem:
                self.sem[semk] = self.es.enter_context(self.nc.semaphore(semk))
                self.count[semk] = 0
                self.dma_chans.add(semk)
            prevc = self.count[semk]
            if prevc > self.know[e].get(semk, 0):
                waits.append((semk, prevc))
                self.know[e][semk] = prevc
            step = 16
        else:
            semk = e
            step = 1
        n = (inc if inc is not None else 1)
        self.count[semk] += step * n
        val = self.count[semk]
        ksnap = dict(self.know[e])
        if dma_ch is None:
            ksnap[semk] = val
        tok = (semk, val, ksnap)
        sem = self.sem[semk]
        wmax = {}
        for sk, sv in waits:
            wmax[sk] = max(wmax.get(sk, 0), sv)
        wlist = [(self.sem[sk], sv) for sk, sv in wmax.items()]
        self.thunks[e].append((wlist, fn, sem, step * n))
        for k in wkeys:
            self.last_write[k] = tok
            self.readers[k] = []
        for k in rkeys:
            self.readers.setdefault(k, []).append(tok)
        self.nops += 1
        if self.DUP_S and e == "S" and dma_ch is None and not _dup:
            return self.op(e, fn, reads, writes, _dup=True)
        return tok

    def dma(self, e, out, in_, reads=(), writes=(), ch=None, **kw):
        if ch is None:
            ch = e + "_" + (self.key_of(writes[0]) if writes else "w")
        return self.op(
            e, lambda eng: eng.dma_start(out=out, in_=in_, **kw),
            reads=reads, writes=writes, dma_ch=ch)

    def barrier(self):
        toks = []
        for e in ENGINES:
            wl = []
            prev = self.count[e]
            if prev > self.know[e].get(e, 0):
                wl.append((self.sem[e], prev))
                self.know[e][e] = prev
            self.count[e] += 1
            ksnap = dict(self.know[e])
            ksnap[e] = self.count[e]
            toks.append((e, self.count[e], ksnap))
            self.thunks[e].append((wl, lambda eng: eng.nop(), self.sem[e], 1))
        for e in ENGINES:
            wl = []
            for semk, val, ksnap in toks:
                if semk == e:
                    continue
                if self.know[e].get(semk, 0) < val:
                    wl.append((self.sem[semk], val))
                    self.know[e][semk] = val
                    for sk, sv in ksnap.items():
                        if self.know[e].get(sk, 0) < sv:
                            self.know[e][sk] = sv
            for dk in self.dma_chans:
                if self.know[e].get(dk, 0) < self.count[dk]:
                    wl.append((self.sem[dk], self.count[dk]))
                    self.know[e][dk] = self.count[dk]
            self.thunks[e].append((wl, None, None, 0))

    def finish(self):
        nc = self.nc
        with nc.Block() as block:
            def section(e):
                def run(eng):
                    for waits, fn, sem, step in self.thunks[e]:
                        for s, v in waits:
                            eng.wait_ge(s, v)
                        if fn is not None:
                            inst = fn(eng)
                            if sem is not None:
                                inst.then_inc(sem, step)
                return run
            block.tensor(section("T"))
            block.vector(section("V"))
            block.scalar(section("S"))
            block.gpsimd(section("P"))
            block.sync(section("Q"))
        self.es.close()



def _wprep(W, m_sz=128):
    M, K = W.shape
    Kr = 64 if K == 64 else 128
    Kp = ((K + Kr - 1) // Kr) * Kr
    if Kp != K:
        Wp = np.zeros((M, Kp), W.dtype)
        Wp[:, :K] = W
        W = Wp
    n_m = M // m_sz
    n_kt = Kp // Kr
    arr = W.reshape(n_m, m_sz, n_kt, Kr).transpose(0, 3, 2, 1)
    return np.ascontiguousarray(arr.reshape(n_m, Kr, n_kt * m_sz))


def _pp(v, n_t):
    return np.ascontiguousarray(v.reshape(n_t, 128).T)


def _host_prep(inp):
    import ml_dtypes
    BF = ml_dtypes.bfloat16
    d = {}
    f32 = np.float32
    d["c_ones"] = np.ones((128, 1), f32)
    d["c_onesb"] = np.ones((128, 1), BF)
    d["c_ones1"] = np.ones((1, 128), f32)
    expsel = np.zeros((128, 16 * 128), BF)
    sel = np.zeros((128, 16 * 128), BF)
    for j in range(16):
        for q in range(128):
            expsel[8 * j + q // 16, j * 128 + q] = 1
            sel[q, j * 128 + 8 * j + q // 16] = 1
    d["c_expsel"] = expsel
    d["c_sel"] = sel
    rep16 = np.zeros((16, 128), BF)
    for q in range(128):
        rep16[q % 16, q] = 1
    d["c_rep16"] = rep16
    gnmask = np.zeros((128, 128), BF)
    gnmaskT = np.zeros((16, 8 * 128), f32)
    for m in range(8):
        for p in range(128):
            h = 2 * m + p // 64
            gnmask[p, m * 16 + h] = 1
            gnmaskT[h, m * 128 + p] = 1
    d["c_gnmask"] = gnmask
    d["c_gnmaskT"] = gnmaskT
    for l in range(2):
        d[f"w_inproj{l}"] = _wprep(np.asarray(inp["L0_attn_in_proj_w"][l], f32))
        d[f"w_outproj{l}"] = _wprep(np.asarray(inp["L0_attn_out_proj_w"][l], f32))
        xp = np.asarray(inp["L0_attn_x_proj_w"][l], f32)
        d[f"w_xpdt{l}"] = _wprep(xp[0:64], m_sz=64)
        d[f"w_xpb{l}"] = _wprep(xp[64:80], m_sz=16)
        d[f"w_xpc{l}"] = _wprep(xp[80:96], m_sz=16)
        d[f"w_dtproj{l}"] = _wprep(np.asarray(inp["L0_attn_dt_proj_w"][l], f32))
        d[f"c_convw{l}"] = np.ascontiguousarray(
            np.asarray(inp["L0_attn_conv_w"][l], f32).reshape(16, 128, 4)
            .transpose(1, 0, 2).reshape(128, 64))
        d[f"c_convb{l}"] = _pp(np.asarray(inp["L0_attn_conv_b"][l], f32), 16)
        d[f"c_dtb{l}"] = _pp(np.asarray(inp["L0_attn_dt_proj_b"][l], f32), 16)
        d[f"c_alog{l}"] = np.ascontiguousarray(
            np.asarray(inp["L0_attn_A_log"][l], f32).reshape(256, 128).T)
        dm = np.zeros((16, 128, 128), f32)
        Dv = np.asarray(inp["L0_attn_D_param"][l], f32).reshape(16, 128)
        for m in range(16):
            dm[m][np.arange(128), np.arange(128)] = Dv[m]
        d[f"c_dmat{l}"] = np.ascontiguousarray(dm.transpose(1, 0, 2).reshape(128, 2048))
    d["w_mlp1"] = _wprep(np.asarray(inp["L0_attn_mlp_w1"], f32))
    d["w_mlp2"] = _wprep(np.asarray(inp["L0_attn_mlp_w2"], f32))
    d["c_mlpb1"] = _pp(np.asarray(inp["L0_attn_mlp_b1"], f32), 16)
    d["c_mlpb2"] = _pp(np.asarray(inp["L0_attn_mlp_b2"], f32), 8)
    d["c_lnw"] = _pp(np.asarray(inp["L0_attn_ln_w"], f32), 8)
    d["c_ln2w"] = _pp(np.asarray(inp["L0_attn_ln2_w"], f32), 8)
    d["w_qgate"] = _wprep(np.asarray(inp["L0_ffn_w_gate"], np.int32))
    d["w_qup"] = _wprep(np.asarray(inp["L0_ffn_w_up"], np.int32))
    d["w_qdown"] = _wprep(np.asarray(inp["L0_ffn_w_down"], np.int32))
    for nm in ["r", "w", "k", "v", "a", "g"]:
        d[f"w_l1{nm}"] = _wprep(np.asarray(inp[f"L1_attn_W_{nm}"], f32))
    d["w_l1out"] = _wprep(np.asarray(inp["L1_attn_W_out"], f32))
    mix = np.stack([np.asarray(inp[f"L1_attn_x_{nm}"], f32).reshape(1024)
                    for nm in ["r", "w", "k", "v", "a", "g"]], 0)
    d["c_mix"] = np.ascontiguousarray(
        mix.reshape(6, 8, 128).transpose(2, 0, 1).reshape(128, 48))
    d["c_bout"] = _pp(np.asarray(inp["L1_attn_b_out"], f32), 8)
    d["c_gnw"] = _pp(np.asarray(inp["L1_attn_gn_w"], f32), 8)
    d["c_gnb"] = _pp(np.asarray(inp["L1_attn_gn_b"], f32), 8)
    d["w_f1"] = _wprep(np.asarray(inp["L1_ffn_w1_w"], f32))
    d["w_f2"] = _wprep(np.asarray(inp["L1_ffn_w2_w"], f32))
    d["w_fh1"] = _wprep(np.asarray(inp["L1_ffn_h1_w"], f32))
    d["w_fh2"] = _wprep(np.asarray(inp["L1_ffn_h2_w"], f32))
    d["w_f3"] = _wprep(np.asarray(inp["L1_ffn_w3_w"], f32))
    d["c_f1s"] = _pp(np.asarray(inp["L1_ffn_w1_s"], f32), 32)
    d["c_f2s"] = _pp(np.asarray(inp["L1_ffn_w2_s"], f32), 32)
    d["c_fh1s"] = _pp(np.asarray(inp["L1_ffn_h1_s"], f32), 32)
    d["c_fh2s"] = _pp(np.asarray(inp["L1_ffn_h2_s"], f32), 32)
    d["c_f3s"] = _pp(np.asarray(inp["L1_ffn_w3_s"], f32), 8)
    return d



class Ctx:

    def __init__(self, p, names):
        nc = p.nc
        self.p = p
        self.dram = {}
        for name, shape, dt in names:
            kind = "Internal"
            if name.startswith(("w_", "c_", "xin")):
                kind = "ExternalInput"
            if name in ("xout", "dbgb", "dbgf"):
                kind = "ExternalOutput"
            self.dram[name] = nc.dram_tensor(name, list(shape), dt, kind=kind).ap()
        self.pbY = p.psum("pbY", [128, 2048], F32)
        self.pbA = p.psum("pbA", [128, 1024], F32)
        self.pbB = p.psum("pbB", [128, 1024], F32)
        self.ones = p.sbuf("ones", [128, 1], F32)
        self.onesb = p.sbuf("onesb", [128, 1], BF16)
        self.ones1 = p.sbuf("ones1", [1, 128], F32)
        p.dma("Q", self.ones[:], self.dram["c_ones"], writes=[self.ones], ch="c")
        p.dma("Q", self.onesb[:], self.dram["c_onesb"], writes=[self.onesb], ch="c")
        p.dma("Q", self.ones1[:], self.dram["c_ones1"], writes=[self.ones1], ch="c")

    def load_const(self, name, shape, dt=F32):
        t = self.p.sbuf(name + "_sb", list(shape), dt)
        self.p.dma("Q", t[:], self.dram[name], writes=[t], ch="c")
        return t


class WStream:

    def __init__(self, ctx, name, kr, fsz, src_dt, want_dt=BF16, tag="", nbufs=2):
        self.p, self.ctx = ctx.p, ctx
        self.name = name
        self.tag = tag or name
        self.kr, self.fsz = kr, fsz
        self.nbufs = nbufs
        self.src_dt, self.want_dt = src_dt, want_dt
        self.src = [self.p.sbuf(f"ws_{self.tag}{i}", [kr, fsz], src_dt) for i in range(nbufs)]
        self.conv = None
        if src_dt != want_dt:
            self.conv = [self.p.sbuf(f"wb_{self.tag}{i}", [kr, fsz], want_dt) for i in range(nbufs)]

    def get(self, m):
        i = m % self.nbufs
        s = self.src[i]
        self.p.dma("Q", s[:], self.ctx.dram[self.name][m], writes=[s], ch=f"w{self.tag}{i}")
        if self.conv is None:
            return s
        c = self.conv[i]
        self.p.op("S", lambda g, s=s, c=c: g.copy(out=c[:], in_=s[:]),
                  reads=[s], writes=[c])
        self.p.op("S", lambda g, s=s, c=c: g.copy(out=c[:], in_=s[:]),
                  reads=[s], writes=[c])
        return c

    def done(self):
        for t in self.src:
            self.p.free(t)
        if self.conv:
            for t in self.conv:
                self.p.free(t)


def mm_group(p, psum_ap, psum_key, pairs, reads, start=True, stop=True):
    def fn(t):
        inst = None
        n = len(pairs)
        for i, (lh, rh) in enumerate(pairs):
            inst = t.matmul(psum_ap, lh, rh,
                            start=(start and i == 0), stop=(stop and i == n - 1))
        return inst
    return p.op("T", fn, reads=reads, writes=[psum_key])


def _rowops_rstd(p, ctx, row_in_psum_ap, rstd_row, sl, dn, eps, reads):
    s = slice(sl * 512, (sl + 1) * 512)
    p.op("S", lambda e: e.activation(out=rstd_row[0:1, s], in_=row_in_psum_ap,
                                     func=AF.Ln, bias=float(eps), scale=1.0 / dn),
         reads=reads, writes=[rstd_row])
    p.op("S", lambda e: e.activation(out=rstd_row[0:1, s], in_=rstd_row[0:1, s],
                                     func=AF.Exp, scale=np.float32(-0.5)),
         reads=[rstd_row], writes=[rstd_row])



def emit_in_ln(p, ctx, xi, b, t0, Ts, abuf, stride, off, u0_dram=None, tag=""):
    nsl = Ts // 512
    xs = p.sbuf(f"xs{tag}", [128, 8 * Ts], F32)
    for dt in range(8):
        p.dma("Q", xs[:, dt * Ts:(dt + 1) * Ts],
              xi[b, dt * 128:(dt + 1) * 128, t0:t0 + Ts], writes=[xs], ch=f"x{dt % 2}")
    mean_r = p.sbuf(f"mean{tag}", [1, Ts], F32)
    rstd_r = p.sbuf(f"rstd{tag}", [1, Ts], F32)
    scr = [p.sbuf(f"lnscr{tag}{i}", [128, 512], F32) for i in range(2)]
    srow = p.sbuf(f"lnrow{tag}", [1, 512], F32)
    srow2 = p.sbuf(f"lnrow2{tag}", [1, 512], F32)
    pbA, pbB, ones, ones1 = ctx.pbA, ctx.pbB, ctx.ones, ctx.ones1
    for sl in range(nsl):
        o = sl * 512
        mm_group(p, pbA[0:1, 0:512], "pbA",
                 [(ones[:, 0:1], xs[:, dt * Ts + o:dt * Ts + o + 512]) for dt in range(8)],
                 reads=[ones, xs])
        p.op("S", lambda e, o=o: e.activation(out=mean_r[0:1, o:o + 512], in_=pbA[0:1, 0:512],
                                              func=AF.Copy, scale=np.float32(1.0 / 1024)),
             reads=["pbA"], writes=[mean_r])
        for dt in range(8):
            sc = scr[dt % 2]
            p.op("S", lambda e, dt=dt, sc=sc, o=o: e.activation(
                out=sc[:], in_=xs[:, dt * Ts + o:dt * Ts + o + 512], func=AF.Square),
                reads=[xs], writes=[sc])
            p.op("T", lambda t, dt=dt, sc=sc: t.matmul(
                pbB[0:1, 0:512], ones[:, 0:1], sc[:], start=(dt == 0), stop=(dt == 7)),
                reads=[ones, sc], writes=["pbB"])
        p.op("S", lambda e, o=o: e.activation(out=srow[0:1, :], in_=mean_r[0:1, o:o + 512],
                                              func=AF.Square), reads=[mean_r], writes=[srow])
        p.op("V", lambda v, o=o: v.scalar_tensor_tensor(
            out=srow[0:1, :], in0=pbB[0:1, 0:512], scalar=1.0 / 1024, in1=srow[0:1, :],
            op0=ALU.mult, op1=ALU.subtract), reads=["pbB", srow], writes=[srow])
        p.op("S", lambda e, o=o: e.activation(out=srow[0:1, :], in_=srow[0:1, :],
                                              func=AF.Ln, bias=np.float32(1e-5)) if False else
             e.activation(out=srow2[0:1, :], in_=srow[0:1, :],
                          func=AF.Ln, bias=np.float32(1e-5)), reads=[srow], writes=[srow2])
        p.op("S", lambda e, o=o: e.activation(out=rstd_r[0:1, o:o + 512], in_=srow2[0:1, :],
                                              func=AF.Exp, scale=np.float32(-0.5)), reads=[srow2], writes=[rstd_r])
    p.op("V", lambda v: v.tensor_mul(out=mean_r[0:1, :], in0=mean_r[0:1, :], in1=rstd_r[0:1, :]),
         reads=[rstd_r, mean_r], writes=[mean_r])
    scw = p.sbuf(f"lnscw{tag}", [128, 1024], F32)
    for hf in range(Ts // 1024):
        h0 = hf * 1024
        p.op("T", lambda t, h0=h0: (t.matmul(pbA[:, 0:512], ones1[:], rstd_r[0:1, h0:h0 + 512],
                                             start=True, stop=True),
                                    t.matmul(pbA[:, 512:1024], ones1[:], rstd_r[0:1, h0 + 512:h0 + 1024],
                                             start=True, stop=True))[-1],
             reads=[ones1, rstd_r], writes=["pbA"])
        p.op("T", lambda t, h0=h0: (t.matmul(pbB[:, 0:512], ones1[:], mean_r[0:1, h0:h0 + 512],
                                             start=True, stop=True),
                                    t.matmul(pbB[:, 512:1024], ones1[:], mean_r[0:1, h0 + 512:h0 + 1024],
                                             start=True, stop=True))[-1],
             reads=[ones1, mean_r], writes=["pbB"])
        for dt in range(8):
            p.op("V", lambda v, dt=dt, h0=h0: v.tensor_mul(
                out=scw[:], in0=xs[:, dt * Ts + h0:dt * Ts + h0 + 1024], in1=pbA[:]),
                reads=[xs, "pbA"], writes=[scw])
            ab = dt * stride + off + h0
            p.op("V", lambda v, ab=ab: v.tensor_sub(
                out=abuf[:, ab:ab + 1024], in0=scw[:], in1=pbB[:]),
                reads=[scw, "pbB"], writes=[abuf])
    if u0_dram is not None:
        for dt in range(8):
            p.dma("Q", u0_dram[dt], abuf[:, dt * stride + off:dt * stride + off + Ts],
                  reads=[abuf], writes=["u0s"], ch=f"u0{dt % 2}")
    for t in scr:
        p.free(t)
    p.free(scw); p.free(srow); p.free(srow2); p.free(mean_r); p.free(rstd_r); p.free(xs)


def emit_out_ln(p, ctx, ybuf, ystride, xi, xo, b, t0, Ts, tag=""):
    nsl = Ts // 512
    mean_r = p.sbuf(f"omean{tag}", [1, Ts], F32)
    rstd_r = p.sbuf(f"orstd{tag}", [1, Ts], F32)
    scr = [p.sbuf(f"oscr{tag}{i}", [128, 512], F32) for i in range(2)]
    srow = p.sbuf(f"orow{tag}", [1, 512], F32)
    srow2 = p.sbuf(f"orow2{tag}", [1, 512], F32)
    pbA, pbB, ones, ones1 = ctx.pbA, ctx.pbB, ctx.ones, ctx.ones1
    onesy = ctx.onesb if ybuf.dtype == BF16 else ctx.ones
    for sl in range(nsl):
        o = sl * 512
        mm_group(p, pbA[0:1, 0:512], "pbA",
                 [(onesy[:, 0:1], ybuf[:, dt * ystride + o:dt * ystride + o + 512]) for dt in range(8)],
                 reads=[onesy, ybuf])
        p.op("S", lambda e, o=o: e.activation(out=mean_r[0:1, o:o + 512], in_=pbA[0:1, 0:512],
                                              func=AF.Copy, scale=np.float32(1.0 / 1024)),
             reads=["pbA"], writes=[mean_r])
        for dt in range(8):
            sc = scr[dt % 2]
            p.op("S", lambda e, dt=dt, sc=sc, o=o: e.activation(
                out=sc[:], in_=ybuf[:, dt * ystride + o:dt * ystride + o + 512], func=AF.Square),
                reads=[ybuf], writes=[sc])
            p.op("T", lambda t, dt=dt, sc=sc: t.matmul(
                pbB[0:1, 0:512], ones[:, 0:1], sc[:], start=(dt == 0), stop=(dt == 7)),
                reads=[ones, sc], writes=["pbB"])
        p.op("S", lambda e, o=o: e.activation(out=srow[0:1, :], in_=mean_r[0:1, o:o + 512],
                                              func=AF.Square), reads=[mean_r], writes=[srow])
        p.op("V", lambda v, o=o: v.scalar_tensor_tensor(
            out=srow[0:1, :], in0=pbB[0:1, 0:512], scalar=1.0 / 1024, in1=srow[0:1, :],
            op0=ALU.mult, op1=ALU.subtract), reads=["pbB", srow], writes=[srow])
        p.op("S", lambda e, o=o: e.activation(out=srow[0:1, :], in_=srow[0:1, :],
                                              func=AF.Ln, bias=np.float32(1e-5)) if False else
             e.activation(out=srow2[0:1, :], in_=srow[0:1, :],
                          func=AF.Ln, bias=np.float32(1e-5)), reads=[srow], writes=[srow2])
        p.op("S", lambda e, o=o: e.activation(out=rstd_r[0:1, o:o + 512], in_=srow2[0:1, :],
                                              func=AF.Exp, scale=np.float32(-0.5)), reads=[srow2], writes=[rstd_r])
    p.op("V", lambda v: v.tensor_mul(out=mean_r[0:1, :], in0=mean_r[0:1, :], in1=rstd_r[0:1, :]),
         reads=[rstd_r, mean_r], writes=[mean_r])
    xtb = [p.sbuf(f"oxs{tag}{i}", [128, Ts], F32) for i in range(2)]
    scw = p.sbuf(f"oscw{tag}", [128, 1024], F32)
    scw2 = p.sbuf(f"oscw2{tag}", [128, 1024], F32)
    for dt in range(8):
        xt = xtb[dt % 2]
        p.dma("Q", xt[:], xi[b, dt * 128:(dt + 1) * 128, t0:t0 + Ts], writes=[xt], ch=f"x{dt % 2}")
        for hf in range(Ts // 1024):
            h0 = hf * 1024
            p.op("T", lambda t, h0=h0: (t.matmul(pbA[:, 0:512], ones1[:], rstd_r[0:1, h0:h0 + 512],
                                                 start=True, stop=True),
                                        t.matmul(pbA[:, 512:1024], ones1[:], rstd_r[0:1, h0 + 512:h0 + 1024],
                                                 start=True, stop=True))[-1],
                 reads=[ones1, rstd_r], writes=["pbA"])
            p.op("T", lambda t, h0=h0: (t.matmul(pbB[:, 0:512], ones1[:], mean_r[0:1, h0:h0 + 512],
                                                 start=True, stop=True),
                                        t.matmul(pbB[:, 512:1024], ones1[:], mean_r[0:1, h0 + 512:h0 + 1024],
                                                 start=True, stop=True))[-1],
                 reads=[ones1, mean_r], writes=["pbB"])
            p.op("V", lambda v, dt=dt, h0=h0: v.tensor_mul(
                out=scw[:], in0=ybuf[:, dt * ystride + h0:dt * ystride + h0 + 1024], in1=pbA[:]),
                reads=[ybuf, "pbA"], writes=[scw])
            p.op("V", lambda v: v.tensor_sub(out=scw2[:], in0=scw[:], in1=pbB[:]),
                 reads=[scw, "pbB"], writes=[scw2])
            p.op("V", lambda v, h0=h0, xt=xt: v.tensor_add(
                out=xt[:, h0:h0 + 1024], in0=scw2[:], in1=xt[:, h0:h0 + 1024]),
                reads=[scw2, xt], writes=[xt])
        p.dma("Q", xo[b, dt * 128:(dt + 1) * 128, t0:t0 + Ts], xt[:],
              reads=[xt], writes=["xoutd"], ch=f"xo{dt % 2}")
    for t in scr:
        p.free(t)
    for t in xtb:
        p.free(t)
    p.free(scw); p.free(scw2); p.free(srow); p.free(srow2); p.free(mean_r); p.free(rstd_r)


def emit_l0ffn(p, ctx, xi, xo):
    SC = 1.0 / 127.0

    pbA, pbB, pbY = ctx.pbA, ctx.pbB, ctx.pbY
    Ts = 2048
    for b in range(2):
        abuf = p.sbuf(f"qa{b}", [128, 8 * Ts], BF16)
        emit_in_ln(p, ctx, xi, b, 0, Ts, abuf, Ts, 0, tag=f"q{b}")
        hbuf = p.sbuf(f"qh{b}", [128, 24 * Ts], BF16)
        sg = p.sbuf(f"qsg{b}", [128, 512], BF16)
        ws_g = WStream(ctx, "w_qgate", 128, 1024, I32, tag=f"qg{b}")
        ws_u = WStream(ctx, "w_qup", 128, 1024, I32, tag=f"qu{b}")
        for m in range(24):
            wg = ws_g.get(m)
            wu = ws_u.get(m)
            for sl in range(4):
                o = sl * 512
                mm_group(p, pbA[:, 0:512], "pbA",
                         [(wg[:, kt * 128:(kt + 1) * 128], abuf[:, kt * Ts + o:kt * Ts + o + 512])
                          for kt in range(8)], reads=[wg, abuf])
                mm_group(p, pbB[:, 0:512], "pbB",
                         [(wu[:, kt * 128:(kt + 1) * 128], abuf[:, kt * Ts + o:kt * Ts + o + 512])
                          for kt in range(8)], reads=[wu, abuf])
                p.op("S", lambda e: e.activation(out=sg[:], in_=pbA[:, 0:512],
                                                 func=AF.Silu, scale=np.float32(SC)),
                     reads=["pbA"], writes=[sg])
                p.op("V", lambda v, m=m, o=o: v.tensor_mul(
                    out=hbuf[:, m * Ts + o:m * Ts + o + 512], in0=sg[:], in1=pbB[:, 0:512]),
                    reads=[sg, "pbB"], writes=[hbuf])
        p.free(abuf); p.free(sg)
        ws_g.done(); ws_u.done()
        ws_d = WStream(ctx, "w_qdown", 128, 3072, I32, nbufs=1, tag=f"qd{b}")
        ybuf = p.sbuf(f"qy{b}", [128, 8 * Ts], BF16)
        for m in range(8):
            wd = ws_d.get(m)
            for sl in range(4):
                o = sl * 512
                mm_group(p, pbY[:, o:o + 512], "pbY",
                         [(wd[:, kt * 128:(kt + 1) * 128], hbuf[:, kt * Ts + o:kt * Ts + o + 512])
                          for kt in range(24)], reads=[wd, hbuf])
            p.op("S", lambda e, m=m: e.activation(out=ybuf[:, m * Ts:(m + 1) * Ts], in_=pbY[:],
                                                  func=AF.Copy, scale=np.float32(SC * SC)),
                 reads=["pbY"], writes=[ybuf])
        p.free(hbuf)
        ws_d.done()
        emit_out_ln(p, ctx, ybuf, Ts, xi, xo, b, 0, Ts, tag=f"q{b}")
        p.free(ybuf)


def emit_l1ffn(p, ctx, xi, xo):
    s1 = ctx.load_const("c_f1s", (128, 32))
    s2 = ctx.load_const("c_f2s", (128, 32))
    sh1 = ctx.load_const("c_fh1s", (128, 32))
    sh2 = ctx.load_const("c_fh2s", (128, 32))
    s3 = ctx.load_const("c_f3s", (128, 8))
    pbA, pbB, pbY = ctx.pbA, ctx.pbB, ctx.pbY
    Ts = 1024
    for si in range(4):
        b, t0 = si // 2, (si % 2) * Ts
        tg = f"f{si}"
        abuf = p.sbuf(f"fa{si}", [128, 8 * Ts], BF16)
        emit_in_ln(p, ctx, xi, b, t0, Ts, abuf, Ts, 0, tag=tg)
        t1 = p.sbuf(f"ft1_{si}", [128, 32 * Ts], BF16)
        sg = p.sbuf(f"fsg{si}", [128, 512], BF16)
        ws1 = WStream(ctx, "w_f1", 128, 1024, F32, tag=f"f1_{si}")
        ws2 = WStream(ctx, "w_f2", 128, 1024, F32, tag=f"f2_{si}")
        for m in range(32):
            w1 = ws1.get(m)
            w2 = ws2.get(m)
            for sl in range(2):
                o = sl * 512
                mm_group(p, pbA[:, 0:512], "pbA",
                         [(w1[:, kt * 128:(kt + 1) * 128], abuf[:, kt * Ts + o:kt * Ts + o + 512])
                          for kt in range(8)], reads=[w1, abuf])
                mm_group(p, pbB[:, 0:512], "pbB",
                         [(w2[:, kt * 128:(kt + 1) * 128], abuf[:, kt * Ts + o:kt * Ts + o + 512])
                          for kt in range(8)], reads=[w2, abuf])
                p.op("S", lambda e, m=m: e.activation(out=sg[:], in_=pbA[:, 0:512],
                                                      func=AF.Silu, scale=s1[:, m:m + 1]),
                     reads=["pbA", s1], writes=[sg])
                p.op("V", lambda v, m=m, o=o: v.scalar_tensor_tensor(
                    out=t1[:, m * Ts + o:m * Ts + o + 512], in0=pbB[:, 0:512],
                    scalar=s2[:, m:m + 1], in1=sg[:], op0=ALU.mult, op1=ALU.mult),
                    reads=["pbB", s2, sg], writes=[t1])
        p.free(abuf)
        ws1.done(); ws2.done()
        t2 = p.sbuf(f"ft2_{si}", [128, 32 * Ts], BF16)
        for hi, (shs, src, dst) in enumerate(((sh1, t1, t2), (sh2, t2, t1))):
            wsh = WStream(ctx, f"w_fh{hi + 1}", 128, 4096, F32, nbufs=1, tag=f"fh{hi}_{si}")
            for m in range(32):
                wh = wsh.get(m)
                for sl in range(2):
                    o = sl * 512
                    mm_group(p, pbY[:, o:o + 512], "pbY",
                             [(wh[:, kt * 128:(kt + 1) * 128], src[:, kt * Ts + o:kt * Ts + o + 512])
                              for kt in range(32)], reads=[wh, src])
                p.op("S", lambda e, m=m, shs=shs, dst=dst: e.activation(
                    out=dst[:, m * Ts:(m + 1) * Ts], in_=pbY[:, 0:Ts],
                    func=AF.Gelu, scale=shs[:, m:m + 1]),
                    reads=["pbY", shs], writes=[dst])
            wsh.done()
        p.free(t2)
        ybuf = p.sbuf(f"fy{si}", [128, 8 * Ts], BF16)
        ws3 = WStream(ctx, "w_f3", 128, 4096, F32, nbufs=1, tag=f"f3_{si}")
        for m in range(8):
            w3 = ws3.get(m)
            for sl in range(2):
                o = sl * 512
                mm_group(p, pbA[:, o:o + 512] if False else pbA[:, 0:512], "pbA",
                         [(w3[:, kt * 128:(kt + 1) * 128], t1[:, kt * Ts + o:kt * Ts + o + 512])
                          for kt in range(32)], reads=[w3, t1])
                p.op("S", lambda e, m=m, o=o: e.activation(
                    out=ybuf[:, m * Ts + o:m * Ts + o + 512], in_=pbA[:, 0:512],
                    func=AF.Identity, bias=0.0, scale=s3[:, m:m + 1]),
                    reads=["pbA", s3], writes=[ybuf])
        ws3.done()
        p.free(t1); p.free(sg)
        emit_out_ln(p, ctx, ybuf, Ts, xi, xo, b, t0, Ts, tag=tg)
        p.free(ybuf)

def emit_l1attn(p, ctx, xi, xo):
    Ts = 2048
    pbA, pbB, pbY = ctx.pbA, ctx.pbB, ctx.pbY
    mix = ctx.load_const("c_mix", (128, 48))
    bout = ctx.load_const("c_bout", (128, 8))
    gnw = ctx.load_const("c_gnw", (128, 8))
    gnb = ctx.load_const("c_gnb", (128, 8))
    gnmask = ctx.load_const("c_gnmask", (128, 128), BF16)
    gnmaskT = ctx.load_const("c_gnmaskT", (16, 1024))
    for b in range(2):
        tg = f"r{b}"
        abuf = p.sbuf(f"ra{b}", [128, 8 * Ts], BF16)
        emit_in_ln(p, ctx, xi, b, 0, Ts, abuf, Ts, 0, tag=tg)
        inp = p.sbuf(f"rinp{b}", [128, 8 * Ts], BF16)
        scrx = p.sbuf(f"rxx{b}", [128, Ts], BF16)
        sev = p.sbuf(f"rsev{b}", [128, 512], BF16)
        scr5 = [p.sbuf(f"rs5{b}{i}", [128, 512], F32) for i in range(3)]

        def build_inp(pi):
            for dt in range(8):
                base = dt * Ts
                p.op("V", lambda v, base=base: v.tensor_scalar_mul(
                    out=scrx[:, 0:1], in0=abuf[:, base:base + 1], scalar1=-1.0),
                    reads=[abuf], writes=[scrx])
                p.op("V", lambda v, base=base: v.tensor_sub(
                    out=scrx[:, 1:Ts], in0=abuf[:, base:base + Ts - 1],
                    in1=abuf[:, base + 1:base + Ts]),
                    reads=[abuf], writes=[scrx])
                p.op("V", lambda v, dt=dt, pi=pi: v.scalar_tensor_tensor(
                    out=inp[:, dt * Ts:(dt + 1) * Ts], in0=scrx[:],
                    scalar=mix[:, pi * 8 + dt:pi * 8 + dt + 1],
                    in1=abuf[:, dt * Ts:(dt + 1) * Ts],
                    op0=ALU.mult, op1=ALU.add),
                    reads=[scrx, mix, abuf], writes=[inp])

        def proj_mm(nm, evac):
            ws = WStream(ctx, f"w_l1{nm}", 128, 1024, F32, tag=f"l1{nm}{b}")
            for m in range(8):
                w = ws.get(m)
                for sl in range(4):
                    o = sl * 512
                    mm_group(p, pbA[:, 0:512], "pbA",
                             [(w[:, kt * 128:(kt + 1) * 128], inp[:, kt * Ts + o:kt * Ts + o + 512])
                              for kt in range(8)], reads=[w, inp])
                    evac(m, o)
            ws.done()

        ub = p.sbuf(f"rub{b}", [128, 8 * Ts], BF16)
        build_inp(2)

        def ev_k(m, o):
            p.op("S", lambda e, m=m, o=o: e.activation(
                out=ub[:, m * Ts + o:m * Ts + o + 512], in_=pbA[:, 0:512], func=AF.Tanh),
                reads=["pbA"], writes=[ub])
        proj_mm("k", ev_k)
        for nm, pi, fn in (("v", 3, AF.Copy), ("a", 4, AF.Sigmoid)):
            build_inp(pi)

            def ev(m, o, fn=fn):
                p.op("S", lambda e, fn=fn: e.activation(out=sev[:], in_=pbA[:, 0:512], func=fn),
                     reads=["pbA"], writes=[sev])
                p.op("V", lambda v, m=m, o=o: v.tensor_mul(
                    out=ub[:, m * Ts + o:m * Ts + o + 512],
                    in0=ub[:, m * Ts + o:m * Ts + o + 512], in1=sev[:]),
                    reads=[ub, sev], writes=[ub])
            proj_mm(nm, ev)
        build_inp(1)
        H = p.sbuf(f"rH{b}", [128, 8 * Ts], BF16)
        dtile = p.sbuf(f"rdec{b}", [128, Ts], BF16)
        ws = WStream(ctx, "w_l1w", 128, 1024, F32, tag=f"l1w{b}")
        for m in range(8):
            w = ws.get(m)
            for sl in range(4):
                o = sl * 512
                mm_group(p, pbA[:, 0:512], "pbA",
                         [(w[:, kt * 128:(kt + 1) * 128], inp[:, kt * Ts + o:kt * Ts + o + 512])
                          for kt in range(8)], reads=[w, inp])
                p.op("S", lambda e: e.activation(out=scr5[0][:], in_=pbA[:, 0:512], func=AF.Abs),
                     reads=["pbA"], writes=[scr5[0]])
                p.op("S", lambda e: e.activation(out=scr5[1][:], in_=scr5[0][:], func=AF.Exp, scale=np.float32(-1.0)),
                     reads=[scr5[0]], writes=[scr5[1]])
                p.op("S", lambda e: e.activation(out=scr5[0][:], in_=scr5[1][:], func=AF.Ln, bias=1.0),
                     reads=[scr5[1]], writes=[scr5[0]])
                p.op("S", lambda e: e.activation(out=scr5[2][:], in_=pbA[:, 0:512], func=AF.Relu),
                     reads=["pbA"], writes=[scr5[2]])
                p.op("V", lambda v: v.tensor_add(out=scr5[0][:], in0=scr5[0][:], in1=scr5[2][:]),
                     reads=[scr5[0], scr5[2]], writes=[scr5[0]])
                p.op("S", lambda e, o=o: e.activation(out=dtile[:, o:o + 512], in_=scr5[0][:],
                                                      func=AF.Exp, scale=np.float32(-1.0)),
                     reads=[scr5[0]], writes=[dtile])
            s = slice(m * Ts, (m + 1) * Ts)
            p.op("V", lambda v, s=s: v.tensor_tensor_scan(
                out=H[:, s], data0=dtile[:], data1=ub[:, s], initial=0.0,
                op0=ALU.mult, op1=ALU.add), reads=[dtile, ub], writes=[H])
        ws.done()
        p.free(dtile); p.free(ub)
        build_inp(0)

        def ev_r(m, o):
            p.op("S", lambda e: e.activation(out=sev[:], in_=pbA[:, 0:512], func=AF.Sigmoid),
                 reads=["pbA"], writes=[sev])
            p.op("V", lambda v, m=m, o=o: v.tensor_mul(
                out=H[:, m * Ts + o:m * Ts + o + 512],
                in0=H[:, m * Ts + o:m * Ts + o + 512], in1=sev[:]),
                reads=[H, sev], writes=[H])
        proj_mm("r", ev_r)
        gb = p.sbuf(f"rgb{b}", [128, 8 * Ts], BF16)
        build_inp(5)
        p.free(abuf); p.free(scrx)

        def ev_g(m, o):
            p.op("S", lambda e, m=m, o=o: e.activation(
                out=gb[:, m * Ts + o:m * Ts + o + 512], in_=pbA[:, 0:512], func=AF.Sigmoid),
                reads=["pbA"], writes=[gb])
        proj_mm("g", ev_g)
        p.free(inp)
        musb = p.sbuf(f"rmu{b}", [16, Ts], F32)
        rssb = p.sbuf(f"rrs{b}", [16, Ts], F32)
        sq = p.sbuf(f"rsq{b}", [128, 512], BF16)
        r16 = p.sbuf(f"r16{b}", [16, 512], F32)
        r16b = p.sbuf(f"r16c{b}", [16, 512], F32)
        for sl in range(4):
            o = sl * 512
            mm_group(p, pbA[0:16, 0:512], "pbA",
                     [(gnmask[:, m * 16:(m + 1) * 16], H[:, m * Ts + o:m * Ts + o + 512])
                      for m in range(8)], reads=[gnmask, H])
            for m in range(8):
                p.op("S", lambda e, m=m, o=o: e.activation(
                    out=sq[:], in_=H[:, m * Ts + o:m * Ts + o + 512], func=AF.Square),
                    reads=[H], writes=[sq])
                p.op("T", lambda t, m=m: t.matmul(
                    pbB[0:16, 0:512], gnmask[:, m * 16:(m + 1) * 16], sq[:],
                    start=(m == 0), stop=(m == 7)), reads=[gnmask, sq], writes=["pbB"])
            p.op("S", lambda e, o=o: e.activation(out=musb[:, o:o + 512], in_=pbA[0:16, 0:512],
                                                  func=AF.Copy, scale=np.float32(1.0 / 64)),
                 reads=["pbA"], writes=[musb])
            p.op("S", lambda e, o=o: e.activation(out=r16[:], in_=musb[:, o:o + 512],
                                                  func=AF.Square), reads=[musb], writes=[r16])
            p.op("V", lambda v: v.scalar_tensor_tensor(
                out=r16[:], in0=pbB[0:16, 0:512], scalar=1.0 / 64, in1=r16[:],
                op0=ALU.mult, op1=ALU.subtract), reads=["pbB", r16], writes=[r16])
            p.op("S", lambda e, o=o: e.activation(out=r16[:], in_=r16[:],
                                                  func=AF.Ln, bias=np.float32(0.00064)) if False else
                 e.activation(out=r16b[:], in_=r16[:], func=AF.Ln, bias=np.float32(0.00064)),
                 reads=[r16], writes=[r16b])
            p.op("S", lambda e, o=o: e.activation(out=rssb[:, o:o + 512], in_=r16b[:],
                                                  func=AF.Exp, scale=np.float32(-0.5)), reads=[r16b], writes=[rssb])
        p.free(sq); p.free(r16); p.free(r16b)
        scw = p.sbuf(f"rscw{b}", [128, 1024], F32)
        for m in range(8):
            for hf in range(2):
                h0 = hf * 1024
                p.op("T", lambda t, m=m, h0=h0: (
                    t.matmul(pbA[:, 0:512], gnmaskT[:, m * 128:(m + 1) * 128],
                             musb[0:16, h0:h0 + 512], start=True, stop=True),
                    t.matmul(pbA[:, 512:1024], gnmaskT[:, m * 128:(m + 1) * 128],
                             musb[0:16, h0 + 512:h0 + 1024], start=True, stop=True))[-1],
                    reads=[gnmaskT, musb], writes=["pbA"])
                p.op("T", lambda t, m=m, h0=h0: (
                    t.matmul(pbB[:, 0:512], gnmaskT[:, m * 128:(m + 1) * 128],
                             rssb[0:16, h0:h0 + 512], start=True, stop=True),
                    t.matmul(pbB[:, 512:1024], gnmaskT[:, m * 128:(m + 1) * 128],
                             rssb[0:16, h0 + 512:h0 + 1024], start=True, stop=True))[-1],
                    reads=[gnmaskT, rssb], writes=["pbB"])
                s = slice(m * Ts + h0, m * Ts + h0 + 1024)
                p.op("V", lambda v, s=s: v.tensor_sub(out=scw[:], in0=H[:, s], in1=pbA[:]),
                     reads=[H, "pbA"], writes=[scw])
                p.op("V", lambda v: v.tensor_mul(out=scw[:], in0=scw[:], in1=pbB[:]),
                     reads=[scw, "pbB"], writes=[scw])
                p.op("V", lambda v, m=m: v.tensor_scalar(
                    out=scw[:], in0=scw[:], scalar1=gnw[:, m:m + 1], scalar2=gnb[:, m:m + 1],
                    op0=ALU.mult, op1=ALU.add), reads=[scw, gnw, gnb], writes=[scw])
                p.op("V", lambda v, s=s: v.tensor_mul(out=H[:, s], in0=scw[:], in1=gb[:, s]),
                     reads=[scw, gb], writes=[H])
        p.free(gb); p.free(musb); p.free(rssb); p.free(scw)
        ybuf = p.sbuf(f"ry{b}", [128, 8 * Ts], F32)
        ws = WStream(ctx, "w_l1out", 128, 1024, F32, tag=f"l1o{b}")
        for m in range(8):
            w = ws.get(m)
            for sl in range(4):
                o = sl * 512
                mm_group(p, pbA[:, 0:512], "pbA",
                         [(w[:, kt * 128:(kt + 1) * 128], H[:, kt * Ts + o:kt * Ts + o + 512])
                          for kt in range(8)], reads=[w, H])
                p.op("S", lambda e, m=m, o=o: e.activation(
                    out=ybuf[:, m * Ts + o:m * Ts + o + 512], in_=pbA[:, 0:512],
                    func=AF.Identity, bias=bout[:, m:m + 1], scale=1.0),
                    reads=["pbA", bout], writes=[ybuf])
        ws.done()
        p.free(H); p.free(sev)
        for t in scr5:
            p.free(t)
        emit_out_ln(p, ctx, ybuf, Ts, xi, xo, b, 0, Ts, tag=tg)
        p.free(ybuf)


def emit_l0attn(p, ctx, xi, xo):
    Ts = 2048
    pbA, pbB, pbY = ctx.pbA, ctx.pbB, ctx.pbY
    expsel = ctx.load_const("c_expsel", (128, 2048), BF16)
    sel = ctx.load_const("c_sel", (128, 2048), BF16)
    rep16 = ctx.load_const("c_rep16", (16, 128), BF16)
    mlpb1 = ctx.load_const("c_mlpb1", (128, 16))
    mlpb2 = ctx.load_const("c_mlpb2", (128, 8))
    lnw = ctx.load_const("c_lnw", (128, 8))
    ln2w = ctx.load_const("c_ln2w", (128, 8))
    tp = p.sbuf("mtilepack", [128, 8192], BF16)
    xct = [tp[:, 0:2048], tp[:, 2048:4096]]
    zld = [tp[:, 4096:6144], tp[:, 6144:8192]]
    cw_l, cb_l, dtb_l, negA_l = [], [], [], []
    for l in range(2):
        cw_l.append(ctx.load_const(f"c_convw{l}", (128, 64)))
        cb_l.append(ctx.load_const(f"c_convb{l}", (128, 16)))
        dtb_l.append(ctx.load_const(f"c_dtb{l}", (128, 16)))
        alog = ctx.load_const(f"c_alog{l}", (128, 256))
        negA = p.sbuf(f"negA{l}", [128, 256], F32)
        p.op("S", lambda e, alog=alog, negA=negA: e.activation(out=negA[:], in_=alog[:], func=AF.Exp),
             reads=[alog], writes=[negA])
        p.op("V", lambda v, negA=negA: v.tensor_scalar_mul(out=negA[:], in0=negA[:], scalar1=-1.0),
             reads=[negA], writes=[negA])
        p.free(alog)
        negA_l.append(negA)
    u0s = ctx.dram["u0s"]
    ures = ctx.dram["ures"]
    zscr = ctx.dram["zscr"]
    xcs = ctx.dram["xcs"]

    def dbg(slot, src_ap, srckey, f32=False):
        if not DEBUG_P0:
            return
        d = ctx.dram["dbgf" if f32 else "dbgb"]
        p.dma("Q", d[slot], src_ap, reads=[srckey], writes=["dbg"], ch="dbg")
    for b in range(2 if not DEBUG_TRUNC else 1):
        ubuf = p.sbuf(f"mu_{b}0", [128, 8 * Ts], BF16)
        emit_in_ln(p, ctx, xi, b, 0, Ts, ubuf, Ts, 0,
                   u0_dram=(u0s if (not DEBUG_TRUNC or DEBUG_TRUNC >= 2) else None), tag=f"m{b}")
        if b == 0:
            dbg(0, ubuf[:, 0:2048], ubuf)
        for l in range(2 if not DEBUG_TRUNC else 1):
            uin = ubuf
            if not DEBUG_TRUNC or DEBUG_TRUNC >= 3:
                for dt in range(8):
                    p.dma("Q", ures[dt], uin[:, dt * Ts:(dt + 1) * Ts], reads=[uin],
                          writes=["ures"], ch=f"ur{dt % 2}")
            xin_b = p.sbuf(f"mxin{b}{l}", [128, 16 * Ts], BF16)
            zst = [p.sbuf(f"mz{b}{l}{i}", [128, Ts], BF16) for i in range(2)]
            ws_ip = WStream(ctx, f"w_inproj{l}", 128, 1024, F32, tag=f"ip{b}{l}")
            for m in range(32):
                if DEBUG_BAR:
                    p.barrier()
                w = ws_ip.get(m)
                if DEBUG_P0 and b == 0 and l == 0 and m == 0:
                    p.dma("Q", ctx.dram["dbgb"][14, :, 0:1024], w[:], reads=[w], writes=["dbg"], ch="dbg")
                    p.dma("Q", ctx.dram["dbgf"][5, :, 0:1024], ws_ip.src[0][:], reads=[ws_ip.src[0]], writes=["dbg"], ch="dbg")
                for sl in range(4):
                    o = sl * 512
                    mm_group(p, pbY[:, o:o + 512], "pbY",
                             [(w[:, kt * 128:(kt + 1) * 128], uin[:, kt * Ts + o:kt * Ts + o + 512])
                              for kt in range(8)], reads=[w, uin])
                if m < 16:
                    p.op("V", lambda e, m=m: e.tensor_copy(
                        out=xin_b[:, m * Ts:(m + 1) * Ts], in_=pbY[:]),
                        reads=["pbY"], writes=[xin_b])
                else:
                    if DEBUG_TRUNC in (1, 2, 3):
                        continue
                    z = zst[m % 2]
                    p.op("V", lambda e, z=z: e.tensor_copy(out=z[:], in_=pbY[:]),
                         reads=["pbY"], writes=[z])
                    p.dma("Q", zscr[m - 16], z[:], reads=[z], writes=["zscr"], ch=f"z{m % 2}")
            ws_ip.done()
            if DEBUG_TRUNC and DEBUG_TRUNC <= 4:
                st16 = p.sbuf("truncst", [128, Ts], F32)
                for mm_ in range(8):
                    p.op("V", lambda e, mm_=mm_: e.tensor_copy(
                        out=st16[:], in_=xin_b[:, mm_ * Ts:(mm_ + 1) * Ts]),
                        reads=[xin_b], writes=[st16])
                    p.dma("Q", xo[0, mm_ * 128:(mm_ + 1) * 128, :], st16[:],
                          reads=[st16], writes=["xoutd"], ch="to")
                    p.op("V", lambda e, mm_=mm_: e.tensor_copy(
                        out=st16[:], in_=xin_b[:, (8 + mm_) * Ts:(9 + mm_) * Ts]),
                        reads=[xin_b], writes=[st16])
                    p.dma("Q", xo[1, mm_ * 128:(mm_ + 1) * 128, :], st16[:],
                          reads=[st16], writes=["xoutd"], ch="to")
                p.barrier()
                return
            if b == 0 and l == 0:
                dbg(1, xin_b[:, 0:2048], xin_b)
                dbg(15, xin_b[:, 2048:4096], xin_b)
                dbg(2, zscr[0], "zscr")
            p.free(uin)
            for t in zst:
                p.free(t)
            cvs = p.sbuf(f"mcvs{b}{l}", [128, Ts], F32)
            cvs2 = p.sbuf(f"mcvs2{b}{l}", [128, Ts], BF16)
            cw, cb = cw_l[l], cb_l[l]
            for dt in range(16):
                base = dt * Ts
                xx = xct[dt % 2]
                p.op("V", lambda v, base=base, dt=dt: v.tensor_scalar(
                    out=cvs[:], in0=xin_b[:, base:base + Ts],
                    scalar1=cw[:, dt * 4 + 3:dt * 4 + 4], scalar2=None, op0=ALU.mult),
                    reads=[xin_b, cw], writes=[cvs])
                for j in (2, 1, 0):
                    s0 = 3 - j
                    p.op("V", lambda v, base=base, dt=dt, j=j, s0=s0: v.scalar_tensor_tensor(
                        out=cvs[:, s0:Ts], in0=xin_b[:, base:base + Ts - s0],
                        scalar=cw[:, dt * 4 + j:dt * 4 + j + 1], in1=cvs[:, s0:Ts],
                        op0=ALU.mult, op1=ALU.add),
                        reads=[xin_b, cw, cvs], writes=[cvs])
                p.op("S", lambda e, dt=dt: e.activation(
                    out=cvs2[:], in_=cvs[:], func=AF.Silu,
                    bias=cb[:, dt:dt + 1]), reads=[cvs, cb], writes=[cvs2])
                p.op("V", lambda v, xx=xx: v.tensor_copy(out=xx[:], in_=cvs2[:]),
                     reads=[cvs2], writes=[xx])
                p.dma("Q", xcs[dt], xx[:], reads=[xx], writes=["xcs"], ch=f"xc{dt % 2}")
            if DEBUG_TRUNC == 5:
                p.barrier()
                for dt16 in range(16):
                    p.dma("P", xo[dt16 // 8, (dt16 % 8) * 128:(dt16 % 8 + 1) * 128, :],
                          xcs[dt16], reads=["xcs"], writes=["xoutd"], ch="to")
                p.barrier()
                return
            if b == 0 and l == 0:
                dbg(3, xcs[0], "xcs")
            p.free(xin_b); p.free(cvs); p.free(cvs2)
            if DEBUG_BAR:
                p.barrier()
            dtr = p.sbuf(f"mdtr{b}{l}", [64, Ts], F32)
            Bacc = p.sbuf(f"mBa{b}{l}", [16, Ts], F32)
            Cacc = p.sbuf(f"mCa{b}{l}", [16, Ts], F32)
            Bsm = p.sbuf(f"mBsm{b}{l}", [16, Ts], BF16)
            Csm = p.sbuf(f"mCsm{b}{l}", [16, Ts], BF16)
            p.op("V", lambda v: v.memset(dtr[:], 0.0), writes=[dtr])
            p.op("V", lambda v: v.memset(Bacc[:], 0.0), writes=[Bacc])
            p.op("V", lambda v: v.memset(Cacc[:], 0.0), writes=[Cacc])
            ws_xd = WStream(ctx, f"w_xpdt{l}", 128, 1024, F32, tag=f"xd{b}{l}")
            ws_xb = WStream(ctx, f"w_xpb{l}", 128, 256, F32, tag=f"xB{b}{l}")
            ws_xc = WStream(ctx, f"w_xpc{l}", 128, 256, F32, tag=f"xC{b}{l}")
            wd, wB, wC = ws_xd.get(0), ws_xb.get(0), ws_xc.get(0)
            if DEBUG_TRUNC == 6:
                snapx = p.sbuf("snapx", [128, Ts], BF16)
                snapw = p.sbuf("snapw", [128, 1024], F32)
            for kt in range(16):
                xx = xct[kt % 2]
                p.dma("Q", xx[:], xcs[kt], reads=["xcs"], writes=[xx], ch=f"xl{kt % 2}")
                if DEBUG_TRUNC == 6 and kt == 0:
                    p.op("V", lambda v: v.tensor_copy(out=snapx[:], in_=xx[:]),
                         reads=[xx], writes=[snapx])
                    p.op("V", lambda v: v.tensor_copy(out=snapw[:], in_=wd[:]),
                         reads=[wd], writes=[snapw])
                def xp1(t, kt=kt, xx=xx):
                    inst = None
                    for sl in range(4):
                        o = sl * 512
                        inst = t.matmul(pbY[0:64, o:o + 512], wd[:, kt * 64:(kt + 1) * 64],
                                        xx[:, o:o + 512], start=True, stop=True)
                    for sl in range(4):
                        o = sl * 512
                        dst = pbA if sl < 2 else pbB
                        inst = t.matmul(dst[0:16, (sl % 2) * 512:(sl % 2) * 512 + 512],
                                        wB[:, kt * 16:(kt + 1) * 16],
                                        xx[:, o:o + 512], start=True, stop=True)
                    return inst
                p.op("T", xp1, reads=[wd, wB, xx], writes=["pbY", "pbA", "pbB"])
                p.op("V", lambda v: v.tensor_add(out=dtr[:], in0=dtr[:], in1=pbY[0:64, :]),
                     reads=[dtr, "pbY"], writes=[dtr])
                p.op("V", lambda v: v.tensor_add(out=Bacc[:, 0:1024], in0=Bacc[:, 0:1024],
                                                 in1=pbA[0:16, :]),
                     reads=[Bacc, "pbA"], writes=[Bacc])
                p.op("V", lambda v: v.tensor_add(out=Bacc[:, 1024:2048], in0=Bacc[:, 1024:2048],
                                                 in1=pbB[0:16, :]),
                     reads=[Bacc, "pbB"], writes=[Bacc])
            for kt in range(16):
                xx = xct[kt % 2]
                p.dma("Q", xx[:], xcs[kt], reads=["xcs"], writes=[xx], ch=f"xl{kt % 2}")
                def xp2(t, kt=kt, xx=xx):
                    inst = None
                    for sl in range(4):
                        o = sl * 512
                        dst = pbA if sl < 2 else pbB
                        inst = t.matmul(dst[0:16, (sl % 2) * 512:(sl % 2) * 512 + 512],
                                        wC[:, kt * 16:(kt + 1) * 16],
                                        xx[:, o:o + 512], start=True, stop=True)
                    return inst
                p.op("T", xp2, reads=[wC, xx], writes=["pbA", "pbB"])
                p.op("V", lambda v: v.tensor_add(out=Cacc[:, 0:1024], in0=Cacc[:, 0:1024],
                                                 in1=pbA[0:16, :]),
                     reads=[Cacc, "pbA"], writes=[Cacc])
                p.op("V", lambda v: v.tensor_add(out=Cacc[:, 1024:2048], in0=Cacc[:, 1024:2048],
                                                 in1=pbB[0:16, :]),
                     reads=[Cacc, "pbB"], writes=[Cacc])
            p.op("V", lambda v: v.tensor_copy(out=Bsm[:], in_=Bacc[:]), reads=[Bacc], writes=[Bsm])
            p.op("V", lambda v: v.tensor_copy(out=Csm[:], in_=Cacc[:]), reads=[Cacc], writes=[Csm])
            p.free(Bacc); p.free(Cacc)
            ws_xd.done(); ws_xb.done(); ws_xc.done()
            Brep = p.sbuf(f"mBrep{b}{l}", [128, Ts], BF16)
            Crep = p.sbuf(f"mCrep{b}{l}", [128, Ts], BF16)
            for rsm, rep in ((Bsm, Brep), (Csm, Crep)):
                for sl in range(4):
                    o = sl * 512
                    p.op("T", lambda t, o=o, rsm=rsm: t.matmul(
                        pbB[:, 0:512], rep16[:], rsm[0:16, o:o + 512], start=True, stop=True),
                        reads=[rep16, rsm], writes=["pbB"])
                    p.op("V", lambda e, o=o, rep=rep: e.tensor_copy(
                        out=rep[:, o:o + 512], in_=pbB[:, 0:512]),
                        reads=["pbB"], writes=[rep])
            p.free(Bsm); p.free(Csm)
            gated = p.sbuf(f"mg{b}{l}", [128, 16 * Ts], BF16)
            dmf = ctx.load_const(f"c_dmat{l}", (128, 2048))
            dmb = p.sbuf(f"dmb{b}{l}", [128, 2048], BF16)
            p.op("S", lambda g, dmf=dmf, dmb=dmb: g.copy(out=dmb[:], in_=dmf[:]),
                 reads=[dmf], writes=[dmb])
            p.free(dmf)
            spakA = p.sbuf(f"mspak{b}{l}", [128, 4 * Ts], F32)
            dt_t = spakA[:, 0 * Ts:1 * Ts]
            dAb = spakA[:, 1 * Ts:2 * Ts]
            binb = spakA[:, 2 * Ts:3 * Ts]
            hb = spakA[:, 3 * Ts:4 * Ts]
            dtu_t = p.sbuf(f"mdtu{b}{l}", [128, Ts], BF16)
            dtb_t = p.sbuf(f"mdtb{b}{l}", [128, Ts], BF16)
            chb = p.sbuf(f"mch{b}{l}", [128, Ts], BF16)
            s5 = [p.sbuf(f"ms5{b}{l}{i}", [128, 512], F32) for i in range(3)]
            yacc = p.sbuf(f"myacc{b}{l}", [128, Ts], F32)
            szb = p.sbuf(f"mszb{b}{l}", [128, Ts], BF16)
            negA = negA_l[l]
            dtb = dtb_l[l]
            ws_dt = WStream(ctx, f"w_dtproj{l}", 64, 128, F32, want_dt=F32, tag=f"dp{b}{l}")
            for m in range(16):
                wdt = ws_dt.get(m)
                xx = xct[m % 2]
                p.dma("Q", xx[:], xcs[m], reads=["xcs"], writes=[xx], ch=f"xl{m % 2}")
                for sl in range(4):
                    o = sl * 512
                    p.op("T", lambda t, o=o, wdt=wdt: t.matmul(
                        pbB[:, 0:512], wdt[:], dtr[:, o:o + 512], start=True, stop=True),
                        reads=[wdt, dtr], writes=["pbB"])
                    p.op("S", lambda e, m=m: e.activation(out=s5[0][:], in_=pbB[:, 0:512],
                                                          func=AF.Abs, bias=dtb[:, m:m + 1]),
                         reads=["pbB", dtb], writes=[s5[0]])
                    p.op("S", lambda e: e.activation(out=s5[1][:], in_=s5[0][:], func=AF.Exp, scale=np.float32(-1.0)),
                         reads=[s5[0]], writes=[s5[1]])
                    p.op("S", lambda e: e.activation(out=s5[0][:], in_=s5[1][:], func=AF.Ln, bias=1.0),
                         reads=[s5[1]], writes=[s5[0]])
                    p.op("S", lambda e, m=m: e.activation(out=s5[2][:], in_=pbB[:, 0:512],
                                                          func=AF.Relu, bias=dtb[:, m:m + 1]),
                         reads=["pbB", dtb], writes=[s5[2]])
                    p.op("V", lambda v, o=o: v.tensor_add(out=dt_t[:, o:o + 512], in0=s5[0][:],
                                                          in1=s5[2][:]),
                         reads=[s5[0], s5[2]], writes=[dt_t])
                if b == 0 and l == 0 and m == 0:
                    dbg(1, dt_t, dt_t, f32=True)
                p.op("V", lambda v, xx=xx: v.tensor_mul(out=dtu_t[:], in0=dt_t[:], in1=xx[:]),
                     reads=[dt_t, xx], writes=[dtu_t])
                p.op("V", lambda e: e.tensor_copy(out=dtb_t[:], in_=dt_t[:]),
                     reads=[dt_t], writes=[dtb_t])
                for j in range(16):
                    tau = m * 16 + j
                    p.op("T", lambda t, j=j: (t.matmul(pbA[:, 0:512], expsel[:, j * 128:(j + 1) * 128],
                                                       dtb_t[:, 0:512], start=True, stop=True),
                                              t.matmul(pbA[:, 512:1024], expsel[:, j * 128:(j + 1) * 128],
                                                       dtb_t[:, 512:1024], start=True, stop=True))[-1],
                         reads=[expsel, dtb_t], writes=["pbA"])
                    p.op("T", lambda t, j=j: (t.matmul(pbB[:, 0:512], expsel[:, j * 128:(j + 1) * 128],
                                                       dtb_t[:, 1024:1536], start=True, stop=True),
                                              t.matmul(pbB[:, 512:1024], expsel[:, j * 128:(j + 1) * 128],
                                                       dtb_t[:, 1536:2048], start=True, stop=True))[-1],
                         reads=[expsel, dtb_t], writes=["pbB"])
                    p.op("S", lambda e, tau=tau: e.activation(out=dAb[:, 0:1024], in_=pbA[:],
                                                              func=AF.Exp, scale=negA[:, tau:tau + 1]),
                         reads=["pbA", negA], writes=[dAb])
                    p.op("S", lambda e, tau=tau: e.activation(out=dAb[:, 1024:2048], in_=pbB[:],
                                                              func=AF.Exp, scale=negA[:, tau:tau + 1]),
                         reads=["pbB", negA], writes=[dAb])
                    p.op("S", lambda e, tau=tau: e.activation(out=dAb[:, 1024:2048], in_=pbB[:],
                                                              func=AF.Exp, scale=negA[:, tau:tau + 1]),
                         reads=["pbB", negA], writes=[dAb])
                    p.op("T", lambda t, j=j: (t.matmul(pbA[:, 0:512], expsel[:, j * 128:(j + 1) * 128],
                                                       dtu_t[:, 0:512], start=True, stop=True),
                                              t.matmul(pbA[:, 512:1024], expsel[:, j * 128:(j + 1) * 128],
                                                       dtu_t[:, 512:1024], start=True, stop=True))[-1],
                         reads=[expsel, dtu_t], writes=["pbA"])
                    p.op("T", lambda t, j=j: (t.matmul(pbB[:, 0:512], expsel[:, j * 128:(j + 1) * 128],
                                                       dtu_t[:, 1024:1536], start=True, stop=True),
                                              t.matmul(pbB[:, 512:1024], expsel[:, j * 128:(j + 1) * 128],
                                                       dtu_t[:, 1536:2048], start=True, stop=True))[-1],
                         reads=[expsel, dtu_t], writes=["pbB"])
                    p.op("V", lambda v: v.scalar_tensor_tensor(
                        out=binb[:, 0:1024], in0=pbA[:], scalar=1.0, in1=Brep[:, 0:1024],
                        op0=ALU.mult, op1=ALU.mult), reads=["pbA", Brep], writes=[binb])
                    p.op("V", lambda v: v.scalar_tensor_tensor(
                        out=binb[:, 1024:2048], in0=pbB[:], scalar=1.0, in1=Brep[:, 1024:2048],
                        op0=ALU.mult, op1=ALU.mult), reads=["pbB", Brep], writes=[binb])
                    p.op("V", lambda v: v.tensor_tensor_scan(
                        out=hb[:], data0=dAb[:], data1=binb[:], initial=0.0,
                        op0=ALU.mult, op1=ALU.add), reads=[dAb, binb], writes=[hb])
                    if b == 0 and l == 0 and m == 0 and j == 0:
                        dbg(2, dAb, dAb, f32=True)
                        dbg(3, binb, binb, f32=True)
                        dbg(4, hb, hb, f32=True)
                    p.op("V", lambda g: g.tensor_mul(out=chb[:], in0=hb[:], in1=Crep[:]),
                         reads=[hb, Crep], writes=[chb])
                    if j == 0:
                        def dfn(t, m=m, xx=xx):
                            inst = None
                            for sl in range(4):
                                o = sl * 512
                                inst = t.matmul(pbY[:, o:o + 512], dmb[:, m * 128:(m + 1) * 128],
                                                xx[:, o:o + 512], start=True, stop=False)
                            return inst
                        p.op("T", dfn, reads=[dmb, xx], writes=["pbY"])

                    def sfn(t, j=j):
                        inst = None
                        for sl in range(4):
                            o = sl * 512
                            inst = t.matmul(pbY[:, o:o + 512], sel[:, j * 128:(j + 1) * 128],
                                            chb[:, o:o + 512], start=False, stop=(j == 15))
                        return inst
                    p.op("T", sfn, reads=[sel, chb], writes=["pbY"])
                z = zld[m % 2]
                p.dma("Q", z[:], zscr[m], reads=["zscr"], writes=[z], ch=f"zl{m % 2}")
                p.op("S", lambda e, z=z: e.activation(out=szb[:], in_=z[:], func=AF.Silu),
                     reads=[z], writes=[szb])
                p.op("V", lambda v, m=m: v.tensor_mul(out=gated[:, m * Ts:(m + 1) * Ts],
                                                      in0=yacc[:], in1=szb[:]),
                     reads=[yacc, szb], writes=[gated])
            if b == 0 and l == 0:
                dbg(6, gated[:, 0:2048], gated)
                dbg(7, Brep[:], Brep)
                dbg(8, Crep[:], Crep)
            ws_dt.done()
            p.free(dtr); p.free(Brep); p.free(Crep); p.free(dmb)
            p.free(spakA); p.free(dtu_t); p.free(dtb_t); p.free(chb); p.free(yacc); p.free(szb)
            for t in s5:
                p.free(t)
            mo = p.sbuf(f"mmo{b}{l}", [128, 8 * Ts], BF16)
            ws_op = WStream(ctx, f"w_outproj{l}", 128, 2048, F32, tag=f"op{b}{l}")
            for m in range(8):
                w = ws_op.get(m)
                for sl in range(4):
                    o = sl * 512
                    mm_group(p, pbA[:, 0:512], "pbA",
                             [(w[:, kt * 128:(kt + 1) * 128], gated[:, kt * Ts + o:kt * Ts + o + 512])
                              for kt in range(16)], reads=[w, gated])
                    p.op("V", lambda e, m=m, o=o: e.tensor_copy(
                        out=mo[:, m * Ts + o:m * Ts + o + 512], in_=pbA[:, 0:512]),
                        reads=["pbA"], writes=[mo])
            if b == 0 and l == 0:
                dbg(9, mo[:, 0:2048], mo)
            ws_op.done()
            p.free(gated)
            hm = p.sbuf(f"mhm{b}{l}", [128, 16 * Ts], BF16)
            ws_m1 = WStream(ctx, "w_mlp1", 128, 1024, F32, tag=f"m1{b}{l}")
            for m in range(16):
                w = ws_m1.get(m)
                for sl in range(4):
                    o = sl * 512
                    mm_group(p, pbA[:, 0:512], "pbA",
                             [(w[:, kt * 128:(kt + 1) * 128], mo[:, kt * Ts + o:kt * Ts + o + 512])
                              for kt in range(8)], reads=[w, mo])
                    p.op("S", lambda e, m=m, o=o: e.activation(
                        out=hm[:, m * Ts + o:m * Ts + o + 512], in_=pbA[:, 0:512],
                        func=AF.Gelu, bias=mlpb1[:, m:m + 1]),
                        reads=["pbA", mlpb1], writes=[hm])
            ws_m1.done()
            p.free(mo)
            mo2 = p.sbuf(f"mmo2{b}{l}", [128, 8 * Ts], BF16)
            ws_m2 = WStream(ctx, "w_mlp2", 128, 2048, F32, tag=f"m2{b}{l}")
            for m in range(8):
                w = ws_m2.get(m)
                for sl in range(4):
                    o = sl * 512
                    mm_group(p, pbA[:, 0:512], "pbA",
                             [(w[:, kt * 128:(kt + 1) * 128], hm[:, kt * Ts + o:kt * Ts + o + 512])
                              for kt in range(16)], reads=[w, hm])
                    p.op("S", lambda e, m=m, o=o: e.activation(
                        out=mo2[:, m * Ts + o:m * Ts + o + 512], in_=pbA[:, 0:512],
                        func=AF.Gelu, bias=mlpb2[:, m:m + 1]),
                        reads=["pbA", mlpb2], writes=[mo2])
            if b == 0 and l == 0:
                dbg(10, hm[:, 0:2048], hm)
                dbg(11, mo2[:, 0:2048], mo2)
            ws_m2.done()
            p.free(hm)
            urb = p.sbuf(f"murb{b}{l}", [128, 8 * Ts], BF16)
            for dt in range(8):
                p.dma("Q", urb[:, dt * Ts:(dt + 1) * Ts], ures[dt], reads=["ures"], writes=[urb], ch=f"ul{dt % 2}")
            for dt in range(8):
                s = slice(dt * Ts, (dt + 1) * Ts)
                p.op("V", lambda v, s=s: v.tensor_add(out=mo2[:, s], in0=mo2[:, s], in1=urb[:, s]),
                     reads=[mo2, urb], writes=[mo2])
            p.free(urb)
            ubuf = p.sbuf(f"mu_{b}{l + 1}", [128, 8 * Ts], BF16)
            _rms_apply(p, ctx, mo2, Ts, lnw, ubuf, Ts, 0, BF16, tag=f"mr{b}{l}")
            if b == 0 and l == 0:
                dbg(12, mo2[:, 0:2048], mo2)
                dbg(13, ubuf[:, 0:2048], ubuf)
            p.free(mo2)
        u0b = p.sbuf(f"mu0b{b}", [128, 8 * Ts], BF16)
        for dt in range(8):
            p.dma("Q", u0b[:, dt * Ts:(dt + 1) * Ts], u0s[dt], reads=["u0s"], writes=[u0b], ch=f"u0l{dt % 2}")
        vb = p.sbuf(f"mvb{b}", [128, 8 * Ts], BF16)
        for dt in range(8):
            s = slice(dt * Ts, (dt + 1) * Ts)
            p.op("V", lambda v, s=s: v.tensor_add(out=vb[:, s], in0=ubuf[:, s], in1=u0b[:, s]),
                 reads=[ubuf, u0b], writes=[vb])
        p.free(ubuf); p.free(u0b)
        _rms_apply(p, ctx, vb, Ts, ln2w, vb, Ts, 0, BF16, tag=f"mf{b}")
        emit_out_ln(p, ctx, vb, Ts, xi, xo, b, 0, Ts, tag=f"m{b}")
        p.free(vb)


def _rms_apply(p, ctx, src, sstride, wpp, dst, dstride, doff, odt, tag=""):
    Ts = sstride
    pbA = ctx.pbA
    ones = ctx.ones
    ones1 = ctx.ones1
    rstd_r = p.sbuf(f"rms_r{tag}", [1, Ts], F32)
    rsc2 = p.sbuf(f"rms_r2{tag}", [1, 512], F32)
    scr = [p.sbuf(f"rms_s{tag}{i}", [128, 512], F32) for i in range(2)]
    for sl in range(Ts // 512):
        o = sl * 512
        for dt in range(8):
            sc = scr[dt % 2]
            p.op("S", lambda e, dt=dt, sc=sc, o=o: e.activation(
                out=sc[:], in_=src[:, dt * Ts + o:dt * Ts + o + 512], func=AF.Square),
                reads=[src], writes=[sc])
            p.op("T", lambda t, dt=dt, sc=sc: t.matmul(
                pbA[0:1, 0:512], ones[:, 0:1], sc[:], start=(dt == 0), stop=(dt == 7)),
                reads=[ones, sc], writes=["pbA"])
        p.op("S", lambda e, o=o: e.activation(out=rsc2[0:1, :], in_=pbA[0:1, 0:512],
                                              func=AF.Ln, bias=np.float32(1.1920929e-07), scale=np.float32(1.0 / 1024)),
             reads=["pbA"], writes=[rsc2])
        p.op("S", lambda e, o=o: e.activation(out=rstd_r[0:1, o:o + 512], in_=rsc2[0:1, :],
                                              func=AF.Exp, scale=np.float32(-0.5)), reads=[rsc2], writes=[rstd_r])
    for hf in range(Ts // 1024):
        h0 = hf * 1024
        p.op("T", lambda t, h0=h0: (t.matmul(pbA[:, 0:512], ones1[:], rstd_r[0:1, h0:h0 + 512],
                                             start=True, stop=True),
                                    t.matmul(pbA[:, 512:1024], ones1[:], rstd_r[0:1, h0 + 512:h0 + 1024],
                                             start=True, stop=True))[-1],
             reads=[ones1, rstd_r], writes=["pbA"])
        for dt in range(8):
            p.op("V", lambda v, dt=dt, h0=h0: v.scalar_tensor_tensor(
                out=dst[:, dt * dstride + doff + h0:dt * dstride + doff + h0 + 1024],
                in0=src[:, dt * Ts + h0:dt * Ts + h0 + 1024],
                scalar=wpp[:, dt:dt + 1], in1=pbA[:], op0=ALU.mult, op1=ALU.mult),
                reads=[src, wpp, "pbA"], writes=[dst])
    for t in scr:
        p.free(t)
    p.free(rstd_r); p.free(rsc2)

DEBUG_P0 = False
DEBUG_BAR = False
DEBUG_TRUNC = 0
_COMMON = [("c_ones", (128, 1), F32), ("c_onesb", (128, 1), BF16), ("c_ones1", (1, 128), F32),
           ("xin", (2, 1024, 2048), F32), ("xout", (2, 1024, 2048), F32)]

_P0_NAMES = _COMMON + [
    ("dbgb", (16, 128, 2048), BF16), ("dbgf", (8, 128, 2048), F32),
    ("w_mlp1", (16, 128, 1024), F32), ("w_mlp2", (8, 128, 2048), F32),
    ("c_mlpb1", (128, 16), F32), ("c_mlpb2", (128, 8), F32),
    ("c_lnw", (128, 8), F32), ("c_ln2w", (128, 8), F32),
    ("c_expsel", (128, 2048), BF16), ("c_sel", (128, 2048), BF16),
    ("c_rep16", (16, 128), BF16),
    ("u0s", (8, 128, 2048), BF16), ("ures", (8, 128, 2048), BF16),
    ("zscr", (16, 128, 2048), BF16), ("xcs", (16, 128, 2048), BF16),
] + [x for l in range(2) for x in [
    (f"w_inproj{l}", (32, 128, 1024), F32), (f"w_outproj{l}", (8, 128, 2048), F32),
    (f"w_xpdt{l}", (1, 128, 1024), F32), (f"w_xpb{l}", (1, 128, 256), F32),
    (f"w_xpc{l}", (1, 128, 256), F32), (f"w_dtproj{l}", (16, 64, 128), F32),
    (f"c_convw{l}", (128, 64), F32), (f"c_convb{l}", (128, 16), F32),
    (f"c_dtb{l}", (128, 16), F32), (f"c_alog{l}", (128, 256), F32),
    (f"c_dmat{l}", (128, 2048), F32)]]

_P1_NAMES = _COMMON + [("w_qgate", (24, 128, 1024), I32), ("w_qup", (24, 128, 1024), I32),
                       ("w_qdown", (8, 128, 3072), I32)]

_P2_NAMES = _COMMON + [(f"w_l1{nm}", (8, 128, 1024), F32)
                       for nm in ["r", "w", "k", "v", "a", "g", "out"]] + [
    ("c_mix", (128, 48), F32), ("c_bout", (128, 8), F32),
    ("c_gnw", (128, 8), F32), ("c_gnb", (128, 8), F32),
    ("c_gnmask", (128, 128), BF16), ("c_gnmaskT", (16, 1024), F32)]

_P3_NAMES = _COMMON + [("w_f1", (32, 128, 1024), F32), ("w_f2", (32, 128, 1024), F32),
                       ("w_fh1", (32, 128, 4096), F32), ("w_fh2", (32, 128, 4096), F32),
                       ("w_f3", (8, 128, 4096), F32),
                       ("c_f1s", (128, 32), F32), ("c_f2s", (128, 32), F32),
                       ("c_fh1s", (128, 32), F32), ("c_fh2s", (128, 32), F32),
                       ("c_f3s", (128, 8), F32)]

_PHASES = [(_P0_NAMES, emit_l0attn), (_P1_NAMES, emit_l0ffn),
           (_P2_NAMES, emit_l1attn), (_P3_NAMES, emit_l1ffn)]


def build_phase(idx):
    names, emitter = _PHASES[idx]
    nc = bass.Bass("TRN2", target_bir_lowering=False, detect_race_conditions=False)
    p = Prog(nc)
    ctx = Ctx(p, names)
    emitter(p, ctx, ctx.dram["xin"], ctx.dram["xout"])
    p.barrier()
    p.finish()
    return nc


def phase_input_names(idx):
    names, _ = _PHASES[idx]
    return [n for n, _, _ in names if n.startswith(("w_", "c_"))]


def kernel(**inputs):
    prep = _host_prep(inputs)
    x = np.ascontiguousarray(
        np.asarray(inputs["L0_attn_x"], np.float32).transpose(0, 2, 1))
    shards = [np.ascontiguousarray(x[2 * c:2 * c + 2]) for c in range(8)]
    for ph in range(4):
        nc = build_phase(ph)
        pins = phase_input_names(ph)
        in_maps = [dict({k: prep[k] for k in pins}, xin=shards[c]) for c in range(8)]
        res = run_bass_kernel_spmd(nc, in_maps, core_ids=list(range(8)))
        shards = [np.asarray(res.results[c]["xout"]) for c in range(8)]
    out = np.concatenate([s.transpose(0, 2, 1) for s in shards], 0)
    return np.ascontiguousarray(out.astype(np.float32))
